# version 23
# baseline (speedup 1.0000x reference)
import os
import numpy as np
import ml_dtypes
import concourse.bass as bass
import concourse.mybir as mybir
from concourse.bass_utils import run_bass_kernel_spmd
from contextlib import ExitStack

KDBG = os.environ.get("KDBG", "") == "1"

F32 = mybir.dt.float32
BF16 = mybir.dt.bfloat16
I32 = mybir.dt.int32
AF = mybir.ActivationFunctionType
ALU = mybir.AluOpType
AX = mybir.AxisListType

NCORES = 8
SEQ = 2048
D = 1024
H = 3072
NSEQ = 2
T = NSEQ * SEQ
DC = D // 128
EC = D // 128
ICH = SEQ // 128
NJB = SEQ // 128
HC = H // 128
C512 = H // 512
TT = 1024
NTT = T // TT

INV127 = float(np.float32(1.0 / 127.0))
INVSQD = float(np.float32(1.0 / 32.0))
EPS = 1e-5
INVD = float(np.float32(1.0 / 1024.0))

ENGINES = ("sync", "gpsimd", "tensor", "vector", "scalar")


class Slot:
    __slots__ = ("w", "r")

    def __init__(self):
        self.w = {}
        self.r = {}

    def note_w(self, sem, val):
        if val > self.w.get(sem, 0):
            self.w[sem] = val

    def note_r(self, sem, val):
        if val > self.r.get(sem, 0):
            self.r[sem] = val

    def set_w(self, sem, val):
        self.w = {sem: val}
        self.r = {}


class Plan:
    def __init__(self, sems):
        self.ops = {e: [] for e in ENGINES}
        self.cnt = {name: 0 for name in sems}
        self.lastw = {e: {} for e in ENGINES}
        self.sems = sems

    def emit(self, e, fn):
        self.ops[e].append(fn)

    def wait(self, e, sem, val):
        if val <= 0 or self.lastw[e].get(sem, 0) >= val:
            return
        self.lastw[e][sem] = val
        h = self.sems[sem]
        self.emit(e, lambda eng, h=h, v=val: eng.wait_ge(h, v))

    def wait_writers(self, e, slot):
        for sem, val in slot.w.items():
            self.wait(e, sem, val)

    def wait_readers(self, e, slot):
        for sem, val in slot.r.items():
            self.wait(e, sem, val)

    def dma(self, e, sem, out_ap, in_ap, **kw):
        self.cnt[sem] += 16
        v = self.cnt[sem]
        h = self.sems[sem]
        self.emit(e, lambda eng, o=out_ap, i=in_ap, h=h, kw=kw:
                  eng.dma_start(out=o, in_=i, **kw).then_inc(h, 16))
        return v

    def op(self, e, fn, inc=None):
        if inc is None:
            self.emit(e, fn)
            return None
        self.cnt[inc] += 1
        v = self.cnt[inc]
        h = self.sems[inc]

        def g(eng, fn=fn, h=h):
            fn(eng).then_inc(h, 1)
        self.emit(e, g)
        return v

    def flush_block(self, nc):
        ops = self.ops
        self.ops = {e: [] for e in ENGINES}
        with nc.Block() as block:
            @block.sync
            def _(eng):
                for f in ops["sync"]:
                    f(eng)

            @block.gpsimd
            def _(eng):
                for f in ops["gpsimd"]:
                    f(eng)

            @block.tensor
            def _(eng):
                for f in ops["tensor"]:
                    f(eng)

            @block.vector
            def _(eng):
                for f in ops["vector"]:
                    f(eng)

            @block.scalar
            def _(eng):
                for f in ops["scalar"]:
                    f(eng)


def build_program():
    nc = bass.Bass("TRN2", target_bir_lowering=False)

    x_d = nc.dram_tensor("x", [T, D], F32, kind="ExternalInput")
    wq_d = nc.dram_tensor("wq", [D, D], F32, kind="ExternalInput")
    wk_d = nc.dram_tensor("wk", [D, D], F32, kind="ExternalInput")
    wv_d = nc.dram_tensor("wv", [D, D], F32, kind="ExternalInput")
    wg_d = nc.dram_tensor("wg", [H, D], I32, kind="ExternalInput")
    wu_d = nc.dram_tensor("wu", [H, D], I32, kind="ExternalInput")
    wd_d = nc.dram_tensor("wd", [D, H], I32, kind="ExternalInput")
    out_d = nc.dram_tensor("out", [T, D], F32, kind="ExternalOutput")

    wgs = nc.dram_tensor("wg_bf", [H, D], BF16, kind="Internal")
    wus = nc.dram_tensor("wu_bf", [H, D], BF16, kind="Internal")
    wds = nc.dram_tensor("wd_bf", [D, H], BF16, kind="Internal")
    wgs2 = nc.dram_tensor("wg_bfT", [D, H], BF16, kind="Internal")
    wus2 = nc.dram_tensor("wu_bfT", [D, H], BF16, kind="Internal")
    wds2 = nc.dram_tensor("wd_bfT", [H, D], BF16, kind="Internal")

    ident_d = nc.inline_tensor(np.eye(128, dtype=ml_dtypes.bfloat16),
                               name="ident128")

    es = ExitStack()

    def sb(name, shape, dt):
        return es.enter_context(nc.sbuf_tensor(name, shape, dt))

    def psum(name, shape, dt):
        return es.enter_context(nc.psum_tensor(name, shape, dt))

    with es:
        sems = {}
        sem_names = (["PE", "DVE", "ACT", "PL", "ID", "WQ", "WK", "WV",
                      "PR0", "PR1", "PR2", "X1ST", "WD", "DBG"]
                     + [f"XP{i}" for i in range(6)]
                     + [f"LN{i}" for i in range(6)]
                     + ["XR0", "XR1", "WG0", "WG1", "WU0", "WU1",
                        "OS0", "OS1", "S20", "S21"])
        for name in sem_names:
            sems[name] = es.enter_context(nc.semaphore(name))
        pl = Plan(sems)

        s_id = sb("s_id", [128, 128], BF16)
        sums2 = sb("sums2", [128, 32], F32)
        sumsq2 = sb("sumsq2", [128, 32], F32)
        mean2 = sb("mean2", [128, 32], F32)
        veps2 = sb("veps2", [128, 32], F32)
        std2 = sb("std2", [128, 32], F32)
        rstd2 = sb("rstd2", [128, 32], F32)
        negmr2 = sb("negmr2", [128, 32], F32)

        P_S = psum("P_S", [128, 2048], F32)
        P_PT = psum("P_PT", [128, 2048], BF16)
        P_ATT = psum("P_ATT", [128, 1024], F32)

        sl_S = [Slot() for _ in range(4)]
        sl_PTh = [Slot(), Slot()]
        sl_A = [Slot() for _ in range(2)]

        x1_store_val = {}
        junk_last = {"v": 0, "s": 0, "g": 0}
        ln2_ready = [0, 0]
        total_prep_lg = 0

        def dbg_dump(name, ap, waits, slot=None):
            if not KDBG:
                return
            shp = [int(d) for d in ap.shape]
            d = nc.dram_tensor("dbg_" + name, shp, ap.dtype,
                               kind="ExternalOutput")
            for s, v in waits:
                pl.wait("sync", s, v)
            v_st = pl.dma("sync", "DBG", d.ap(), ap)
            if slot is not None:
                slot.note_r("DBG", v_st)

        def _clr(eng):
            for h in sems.values():
                eng.sem_clear(h)
        pl.emit("sync", _clr)
        pl.flush_block(nc)

        ab = ExitStack()

        def sb_ab(name, shape, dt):
            return ab.enter_context(nc.sbuf_tensor(name, shape, dt))

        with ab:
            y1 = sb_ab("y1", [128, 8 * 512], BF16)
            q_f = sb_ab("q_f", [128, 8 * 2048], BF16)
            k_f = sb_ab("k_f", [128, 8 * 2048], BF16)
            v_t = sb_ab("v_t", [128, 16 * 1024], BF16)
            wq_sb = sb_ab("wq_sb", [128, 8 * 1024], BF16)
            wk_sb = sb_ab("wk_sb", [128, 8 * 1024], BF16)
            wv_sb = sb_ab("wv_sb", [128, 8 * 1024], BF16)
            xpool = sb_ab("xpool", [128, 6 * 1024], F32)
            stage2 = sb_ab("stage2", [128, 2 * 1024], BF16)
            ya = sb_ab("ya", [128, 4 * 1024], BF16)
            junk_v = sb_ab("junk_v", [128, 1024], BF16)
            junk_s = sb_ab("junk_s", [128, 1024], BF16)
            pprime = sb_ab("pprime", [128, 2048], BF16)
            pt_sb = sb_ab("pt_sb", [128, 2048], BF16)
            x1st = sb_ab("x1st", [128, 1024], F32)
            sums1 = sb_ab("sums1", [128, 4], F32)
            sumsq1 = sb_ab("sumsq1", [128, 4], F32)
            mean1 = sb_ab("mean1", [128, 4], F32)
            tmp1 = sb_ab("tmp1", [128, 4], F32)
            veps1 = sb_ab("veps1", [128, 4], F32)
            std1 = sb_ab("std1", [128, 4], F32)
            rstd1 = sb_ab("rstd1", [128, 4], F32)
            negmr1 = sb_ab("negmr1", [128, 4], F32)
            negmax = sb_ab("negmax", [128, 1], F32)
            nmb = sb_ab("nmb", [128, 1], F32)
            zsum = sb_ab("zsum", [128, 1], F32)
            rz = sb_ab("rz", [128, 1], F32)

            sl_x = [Slot() for _ in range(6)]
            sl_s2 = [Slot() for _ in range(2)]
            sl_ya = [Slot() for _ in range(4)]
            sl_y1 = Slot()
            sl_qf = Slot()
            sl_kf = Slot()
            sl_vt = Slot()
            sl_wq = Slot()
            sl_wk = Slot()
            sl_wv = Slot()
            sl_pp = Slot()
            sl_ptsb = Slot()
            sl_x1st = Slot()
            sl_sumsq1 = Slot()
            sl_veps1 = Slot()
            sl_std1 = Slot()
            sl_rn1 = Slot()
            sl_nmb = Slot()
            sl_z = Slot()
            sl_sumsq2 = Slot()
            sl_std2 = Slot()

            id_ready = pl.dma("sync", "ID", s_id[:, :], ident_d.ap())

            w_ready = {}
            for wname, wsem, wdram, wsb, wslot in (
                    ("q", "WQ", wq_d, wq_sb, sl_wq),
                    ("k", "WK", wk_d, wk_sb, sl_wk),
                    ("v", "WV", wv_d, wv_sb, sl_wv)):
                v = 0
                for dc in range(DC):
                    v = pl.dma("gpsimd", wsem,
                               wsb[:, dc * 1024:(dc + 1) * 1024],
                               wdram.ap()[dc * 128:(dc + 1) * 128, :])
                    wslot.note_w(wsem, v)
                w_ready[wname] = (wsem, v)

            p1_done = {"wg": [("PR0", 64)], "wu": [("PR1", 64)],
                       "wd": [("PR2", 64)]}
            p1_chunks = (
                [("PR0", wg_d, wgs, k * (H // 4), (k + 1) * (H // 4))
                 for k in range(4)]
                + [("PR1", wu_d, wus, k * (H // 4), (k + 1) * (H // 4))
                   for k in range(4)]
                + [("PR2", wd_d, wds, k * (D // 4), (k + 1) * (D // 4))
                   for k in range(4)]
            )
            p1_ptr = [0]

            def emit_p1(k):
                for _ in range(k):
                    if p1_ptr[0] >= len(p1_chunks):
                        return
                    sem, src, dst, r0, r1 = p1_chunks[p1_ptr[0]]
                    p1_ptr[0] += 1
                    pl.wait("gpsimd", "PE", pl.cnt["PE"])
                    pl.dma("gpsimd", sem, dst.ap()[r0:r1, :],
                           src.ap()[r0:r1, :])

            p2_jobs = []
            for dc in range(DC):
                for hp in range(3):
                    p2_jobs.append((wgs, wgs2, p1_done["wg"],
                                    hp * 1024, dc * 128))
            for dc in range(DC):
                for hp in range(3):
                    p2_jobs.append((wus, wus2, p1_done["wu"],
                                    hp * 1024, dc * 128))
            for hc in range(HC):
                p2_jobs.append((wds, wds2, p1_done["wd"], 0, hc * 128))
            p2_ptr = [0]
            p2_done = []

            def emit_p2(k):
                for _ in range(k):
                    if p2_ptr[0] >= len(p2_jobs):
                        return
                    j = p2_ptr[0]
                    p2_ptr[0] += 1
                    src, dst, deps, r0, c0 = p2_jobs[j]
                    p = j % 2
                    sem = f"S2{p}"
                    sl = sl_s2[p]
                    for dsem, dval in deps:
                        pl.wait("sync", dsem, dval)
                    pl.wait_readers("sync", sl)
                    v = pl.dma("sync", sem,
                               stage2[:, p * 1024:(p + 1) * 1024],
                               src.ap()[r0:r0 + 1024, c0:c0 + 128],
                               transpose=True)
                    sl.set_w(sem, v)
                    pl.wait("gpsimd", sem, v)
                    v = pl.dma("gpsimd", sem,
                               dst.ap()[c0:c0 + 128, r0:r0 + 1024],
                               stage2[:, p * 1024:(p + 1) * 1024])
                    sl.note_r(sem, v)
            p2_total = len(p2_jobs)

            x_use = [0]

            def load_x_tile(row0):
                g = x_use[0]
                x_use[0] += 1
                slot = g % 6
                sem = f"XP{slot}"
                sl = sl_x[slot]
                pl.wait_readers("sync", sl)
                v = pl.dma("sync", sem,
                           xpool[:, slot * 1024:(slot + 1) * 1024],
                           x_d.ap()[row0:row0 + 128, :])
                sl.set_w(sem, v)
                return slot, (sem, v)

            qkv_ps = [0]

            ln1_state = {}

            def emit_ln_stats(s, ib):
                xslots = []
                for ts in range(4):
                    row0 = s * SEQ + ib * 512 + ts * 128
                    xslots.append(load_x_tile(row0))
                for ts, (slot, vld) in enumerate(xslots):
                    xt = xpool[:, slot * 1024:(slot + 1) * 1024]
                    pl.wait("scalar", *vld)
                    pl.wait("scalar", "ACT", junk_last["s"])
                    v = pl.op("scalar", lambda e, xt=xt, ts=ts:
                              e.activation(
                                  junk_s[:, :], xt, AF.Identity,
                                  accum_out=sums1[:, ts:ts + 1]), inc="ACT")
                    junk_last["s"] = v
                    sl_x[slot].note_r("ACT", v)
                    pl.wait("scalar", *vld)
                    pl.wait_readers("scalar", sl_sumsq1)
                    pl.wait("scalar", "ACT", junk_last["s"])
                    v = pl.op("scalar", lambda e, xt=xt, ts=ts:
                              e.activation(
                                  junk_s[:, :], xt, AF.Square,
                                  accum_out=sumsq1[:, ts:ts + 1]), inc="ACT")
                    junk_last["s"] = v
                    sl_x[slot].note_r("ACT", v)
                    sl_sumsq1.note_w("ACT", v)
                ln1_state[("stats", s, ib)] = xslots

            def emit_ln_mid(s, ib):
                xslots = ln1_state.pop(("stats", s, ib))
                pl.wait_writers("vector", sl_sumsq1)
                pl.wait_readers("vector", sl_veps1)
                pl.wait("vector", "ACT", pl.cnt["ACT"])
                v_mean = pl.op("vector", lambda e: e.tensor_scalar_mul(
                    mean1[:, :], sums1[:, :], INVD), inc="DVE")
                v_ve = pl.op("vector", lambda e: e.tensor_scalar(
                    veps1[:, :], sumsq1[:, :], INVD, EPS, ALU.mult,
                    ALU.add), inc="DVE")
                pl.wait("vector", "DVE", v_mean)
                v_tmp = pl.op("vector", lambda e: e.tensor_mul(
                    tmp1[:, :], mean1[:, :], mean1[:, :]), inc="DVE")
                pl.wait("vector", "DVE", v_tmp)
                v_fin = pl.op("vector", lambda e: e.tensor_sub(
                    veps1[:, :], veps1[:, :], tmp1[:, :]), inc="DVE")
                sl_sumsq1.note_r("DVE", v_fin)
                sl_veps1.set_w("DVE", v_fin)
                pl.wait("scalar", "DVE", v_fin)
                pl.wait_readers("scalar", sl_std1)
                v_std = pl.op("scalar", lambda e: e.activation(
                    std1[:, :], veps1[:, :], AF.Sqrt), inc="ACT")
                sl_veps1.note_r("ACT", v_std)
                sl_std1.set_w("ACT", v_std)
                pl.wait("vector", "ACT", v_std)
                pl.wait_readers("vector", sl_rn1)
                v_rc = pl.op("vector", lambda e: e.reciprocal(
                    rstd1[:, :], std1[:, :]), inc="DVE")
                pl.wait("vector", "DVE", v_rc)
                v_nr = pl.op("vector", lambda e: e.scalar_tensor_tensor(
                    negmr1[:, :], mean1[:, :], -1.0, rstd1[:, :],
                    ALU.mult, ALU.mult), inc="DVE")
                sl_std1.note_r("DVE", v_nr)
                sl_rn1.set_w("DVE", v_nr)
                ln1_state[("mid", s, ib)] = (xslots, v_nr)

            def emit_ln_norm(s, ib):
                xslots, v_nr = ln1_state.pop(("mid", s, ib))
                items = []
                for ts, (slot, vld) in enumerate(xslots):
                    xt = xpool[:, slot * 1024:(slot + 1) * 1024]
                    ysl = (ib * 4 + ts) % 4
                    yt = ya[:, ysl * 1024:(ysl + 1) * 1024]
                    pl.wait("scalar", "DVE", v_nr)
                    pl.wait_readers("scalar", sl_ya[ysl])
                    v_norm = pl.op("scalar", lambda e, xt=xt, yt=yt, ts=ts:
                                   e.activation(
                                       yt, xt, AF.Identity,
                                       bias=negmr1[:, ts:ts + 1],
                                       scale=rstd1[:, ts:ts + 1]), inc="ACT")
                    sl_x[slot].note_r("ACT", v_norm)
                    sl_ya[ysl].set_w("ACT", v_norm)
                    sl_rn1.note_r("ACT", v_norm)
                    items.append((ysl, v_norm))
                ln1_state[(s, ib)] = (items, v_nr)

            def emit_ln_finish(s, ib):
                emit_ln_mid(s, ib)
                emit_ln_norm(s, ib)

            def emit_ln_chain(s, ib):
                emit_ln_stats(s, ib)
                emit_ln_finish(s, ib)

            def emit_trans_evac(s, ib):
                items, v_nr = ln1_state.pop((s, ib))
                ev_max = 0
                for ts, (ysl, v_norm) in enumerate(items):
                    yt = ya[:, ysl * 1024:(ysl + 1) * 1024]
                    half = ts % 2
                    pt = P_PT[:, half * 1024:(half + 1) * 1024]
                    pl.wait("tensor", "ACT", v_norm)
                    pl.wait("tensor", "ID", id_ready)
                    pl.wait_readers("tensor", sl_PTh[half])
                    v_tr = 0
                    for dc in range(DC):
                        inc = "PE" if dc == DC - 1 else None
                        vv = pl.op("tensor", lambda e, yt=yt, dc=dc, pt=pt:
                                   e.transpose(
                                       pt[:, dc * 128:(dc + 1) * 128],
                                       yt[:, dc * 128:(dc + 1) * 128],
                                       s_id[:, :]), inc=inc)
                        if vv is not None:
                            v_tr = vv
                    sl_PTh[half].set_w("PE", v_tr)
                    sl_ya[ysl].note_r("PE", v_tr)
                    pl.wait("vector", "PE", v_tr)
                    pl.wait_readers("vector", sl_y1)
                    src = pt.rearrange("p (dc i) -> p dc i", dc=8)
                    dst = y1[:, :].rearrange(
                        "p (dc i) -> p dc i",
                        dc=8)[:, :, ts * 128:(ts + 1) * 128]
                    v_ev = pl.op("vector", lambda e, src=src, dst=dst:
                                 e.tensor_copy(dst, src), inc="DVE")
                    sl_PTh[half].note_r("DVE", v_ev)
                    sl_y1.note_w("DVE", v_ev)
                    ev_max = max(ev_max, v_ev)
                if KDBG and s == 0 and ib == 0:
                    dbg_dump("y1", y1[:, :], [("DVE", ev_max)], sl_y1)
                    dbg_dump("rstd1", rstd1[:, :], [("DVE", v_nr)], sl_rn1)
                    dbg_dump("negmr1", negmr1[:, :], [("DVE", v_nr)], sl_rn1)
                return ev_max

            att_ps = [0]

            def mm_group(lhs, rhs, dest_ap, dest_slot, read_slots,
                         att_banks=False):
                if att_banks:
                    psl = att_ps[0] % 2
                    att_ps[0] += 1
                    ps_ap = P_ATT[:, psl * 512:(psl + 1) * 512]
                    slot_obj = sl_A[psl]
                else:
                    psl = qkv_ps[0] % 4
                    qkv_ps[0] += 1
                    ps_ap = P_S[:, psl * 512:(psl + 1) * 512]
                    slot_obj = sl_S[psl]
                pl.wait_readers("tensor", slot_obj)
                n = len(lhs)
                v_mm = 0
                for i, (la, ra) in enumerate(zip(lhs, rhs)):
                    inc = "PE" if i == n - 1 else None
                    vv = pl.op("tensor", lambda e, la=la, ra=ra, i=i,
                               ps_ap=ps_ap, n=n:
                               e.matmul(ps_ap, la, ra, start=(i == 0),
                                        stop=(i == n - 1)), inc=inc)
                    if vv is not None:
                        v_mm = vv
                slot_obj.set_w("PE", v_mm)
                for rs in read_slots:
                    rs.note_r("PE", v_mm)
                pl.wait("vector", "PE", v_mm)
                pl.wait_readers("vector", dest_slot)
                v_ev = pl.op("vector", lambda e, d=dest_ap, s=ps_ap:
                             e.tensor_copy(d, s), inc="DVE")
                slot_obj.note_r("DVE", v_ev)
                dest_slot.note_w("DVE", v_ev)

            def phase_a(s):
                if (s, 0) not in ln1_state:
                    emit_ln_chain(s, 0)
                for ib in range(4):
                    ev_max = emit_trans_evac(s, ib)
                    if s == 0 and ib >= 1:
                        emit_p1(1)
                    if s == 1:
                        emit_p2(4)
                    if ib + 1 < 4:
                        emit_ln_chain(s, ib + 1)
                    pl.wait("tensor", "DVE", ev_max)
                    pl.wait("tensor", *w_ready["q"])
                    for ec in range(EC):
                        o = ec * 2048 + ib * 512
                        mm_group(
                            [wq_sb[:, dc * 1024 + ec * 128:
                                   dc * 1024 + ec * 128 + 128]
                             for dc in range(DC)],
                            [y1[:, dc * 512:(dc + 1) * 512] for dc in range(DC)],
                            q_f[:, o:o + 512], sl_qf, [sl_wq, sl_y1])
                    pl.wait("tensor", *w_ready["k"])
                    for ec in range(EC):
                        o = ec * 2048 + ib * 512
                        mm_group(
                            [wk_sb[:, dc * 1024 + ec * 128:
                                   dc * 1024 + ec * 128 + 128]
                             for dc in range(DC)],
                            [y1[:, dc * 512:(dc + 1) * 512] for dc in range(DC)],
                            k_f[:, o:o + 512], sl_kf, [sl_wk, sl_y1])
                    if ib == 3:
                        scores_group(s, 0)
                        emit_rmnmb(s, 0)
                        emit_exp(0)
                    pl.wait("tensor", *w_ready["v"])
                    for isub in range(4):
                        ic = ib * 4 + isub
                        for et in range(2):
                            o = ic * 1024 + et * 512
                            mm_group(
                                [y1[:, dc * 512 + isub * 128:
                                    dc * 512 + isub * 128 + 128]
                                 for dc in range(DC)],
                                [wv_sb[:, dc * 1024 + et * 512:
                                       dc * 1024 + et * 512 + 512]
                                 for dc in range(DC)],
                                v_t[:, o:o + 512], sl_vt, [sl_wv, sl_y1],
                                att_banks=(ib == 3))

            def scores_group(s, jb):
                pl.wait_writers("tensor", sl_qf)
                pl.wait_writers("tensor", sl_kf)
                for it in range(4):
                    sls = sl_S[it]
                    pl.wait_readers("tensor", sls)
                    v_mm = 0
                    for ec in range(EC):
                        inc = "PE" if ec == EC - 1 else None
                        vv = pl.op("tensor", lambda e, ec=ec, it=it, jb=jb:
                                   e.matmul(
                                       P_S[:, it * 512:(it + 1) * 512],
                                       k_f[:, ec * 2048 + jb * 128:
                                           ec * 2048 + jb * 128 + 128],
                                       q_f[:, ec * 2048 + it * 512:
                                           ec * 2048 + it * 512 + 512],
                                       start=(ec == 0), stop=(ec == EC - 1)),
                                   inc=inc)
                        if vv is not None:
                            v_mm = vv
                    sls.set_w("PE", v_mm)
                    sl_qf.note_r("PE", v_mm)
                    sl_kf.note_r("PE", v_mm)

            rmst = {}
            expst = {}

            def emit_rmnmb(s, jb):
                    row0 = s * SEQ + jb * 128
                    xinfo = load_x_tile(row0)
                    for it in range(4):
                        pl.wait_writers("vector", sl_S[it])
                    v = pl.op("vector", lambda e: e.reduce_max(
                        negmax[:, :], P_S[:, :], axis=AX.X, negate=True),
                        inc="DVE")
                    for it in range(4):
                        sl_S[it].note_r("DVE", v)
                    pl.wait_readers("vector", sl_nmb)
                    pl.wait("vector", "DVE", v)
                    v_nmb = pl.op("vector", lambda e: e.tensor_scalar_mul(
                        nmb[:, :], negmax[:, :], INVSQD), inc="DVE")
                    sl_nmb.set_w("DVE", v_nmb)
                    rmst[jb] = (xinfo, v_nmb)

            def emit_exp(jb):
                    v_nmb = rmst[jb][1]
                    pl.wait("scalar", "DVE", v_nmb)
                    pl.wait_readers("scalar", sl_pp)
                    pl.wait_readers("scalar", sl_z)
                    v_exp = pl.op("scalar", lambda e: e.activation(
                        pprime[:, :], P_S[:, :], AF.Exp, bias=nmb[:, :],
                        scale=INVSQD, accum_out=zsum[:, :]), inc="ACT")
                    for it in range(4):
                        sl_S[it].note_r("ACT", v_exp)
                    sl_nmb.note_r("ACT", v_exp)
                    sl_pp.set_w("ACT", v_exp)
                    sl_z.set_w("ACT", v_exp)
                    expst[jb] = v_exp

            def phase_b(s):
                for jb in range(NJB):
                    col = s * NJB + jb
                    row0 = s * SEQ + jb * 128
                    (xslot, vxl), v_nmb = rmst.pop(jb)
                    v_exp = expst.pop(jb)
                    xt = xpool[:, xslot * 1024:(xslot + 1) * 1024]
                    if KDBG and s == 0 and jb == 0:
                        dbg_dump("nmb", nmb[:, :], [("DVE", v_nmb)], sl_nmb)
                        dbg_dump("pprime", pprime[:, :], [("ACT", v_exp)],
                                 sl_pp)
                        dbg_dump("zsum", zsum[:, :], [("ACT", v_exp)], sl_z)
                    pl.wait("vector", "ACT", v_exp)
                    v_rz = pl.op("vector", lambda e: e.reciprocal(
                        rz[:, :], zsum[:, :]), inc="DVE")
                    sl_z.note_r("DVE", v_rz)
                    pl.wait("tensor", "ACT", v_exp)
                    pl.wait_readers("tensor", sl_PTh[0])
                    pl.wait_readers("tensor", sl_PTh[1])
                    v_tr = 0
                    for ic in range(ICH):
                        inc = "PE" if ic == ICH - 1 else None
                        vv = pl.op("tensor", lambda e, ic=ic: e.transpose(
                            P_PT[:, ic * 128:(ic + 1) * 128],
                            pprime[:, ic * 128:(ic + 1) * 128],
                            s_id[:, :]), inc=inc)
                        if vv is not None:
                            v_tr = vv
                    sl_PTh[0].set_w("PE", v_tr)
                    sl_PTh[1].set_w("PE", v_tr)
                    sl_pp.note_r("PE", v_tr)
                    pl.wait("vector", "PE", v_tr)
                    pl.wait_readers("vector", sl_ptsb)
                    v_pte = pl.op("vector", lambda e: e.tensor_copy(
                        pt_sb[:, :], P_PT[:, :]), inc="DVE")
                    sl_PTh[0].note_r("DVE", v_pte)
                    sl_PTh[1].note_r("DVE", v_pte)
                    sl_ptsb.set_w("DVE", v_pte)
                    if jb + 1 < NJB:
                        scores_group(s, jb + 1)
                        emit_rmnmb(s, jb + 1)
                        emit_exp(jb + 1)
                    pl.wait("tensor", "DVE", v_pte)
                    att_max = 0
                    for et in range(2):
                        sla = sl_A[et]
                        pl.wait_readers("tensor", sla)
                        v_mm = 0
                        for ic in range(ICH):
                            inc = "PE" if ic == ICH - 1 else None
                            vv = pl.op("tensor", lambda e, ic=ic, et=et:
                                       e.matmul(
                                           P_ATT[:, et * 512:(et + 1) * 512],
                                           pt_sb[:, ic * 128:(ic + 1) * 128],
                                           v_t[:, ic * 1024 + et * 512:
                                               ic * 1024 + et * 512 + 512],
                                           start=(ic == 0),
                                           stop=(ic == ICH - 1)), inc=inc)
                            if vv is not None:
                                v_mm = vv
                        sla.set_w("PE", v_mm)
                        sl_ptsb.note_r("PE", v_mm)
                        sl_vt.note_r("PE", v_mm)
                        att_max = max(att_max, v_mm)
                    pl.wait("vector", "PE", att_max)
                    pl.wait("vector", *vxl)
                    pl.wait("vector", "DVE", v_rz)
                    pl.wait_readers("vector", sl_x1st)
                    v_x1 = pl.op("vector", lambda e, xt=xt:
                                 e.scalar_tensor_tensor(
                                     x1st[:, :], P_ATT[:, :], rz[:, :], xt,
                                     ALU.mult, ALU.add), inc="DVE")
                    for et in range(2):
                        sl_A[et].note_r("DVE", v_x1)
                    sl_x[xslot].note_r("DVE", v_x1)
                    sl_x1st.set_w("DVE", v_x1)
                    if KDBG and s == 0 and jb == 0:
                        dbg_dump("ptsb", pt_sb[:, :], [("DVE", v_pte)],
                                 sl_ptsb)
                        dbg_dump("x1st", x1st[:, :], [("DVE", v_x1)], sl_x1st)
                    pl.wait("sync", "DVE", v_x1)
                    v_st = pl.dma("sync", "X1ST",
                                  out_d.ap()[row0:row0 + 128, :], x1st[:, :])
                    sl_x1st.note_r("X1ST", v_st)
                    x1_store_val[(s, jb)] = v_st
                    if s == 0 and jb <= 8:
                        emit_p1(1)
                    if s == 0 and jb >= 6:
                        emit_p2(2)
                    elif s == 1:
                        emit_p2(3)
                    pl.wait("scalar", "DVE", v_x1)
                    pl.wait("scalar", "ACT", junk_last["s"])
                    v = pl.op("scalar", lambda e, col=col: e.activation(
                        junk_s[:, :], x1st[:, :], AF.Identity,
                        accum_out=sums2[:, col:col + 1]), inc="ACT")
                    junk_last["s"] = v
                    sl_x1st.note_r("ACT", v)
                    pl.wait("scalar", "DVE", v_x1)
                    pl.wait_readers("scalar", sl_sumsq2)
                    pl.wait("scalar", "ACT", junk_last["s"])
                    v_sq = pl.op("scalar", lambda e, col=col: e.activation(
                        junk_s[:, :], x1st[:, :], AF.Square,
                        accum_out=sumsq2[:, col:col + 1]), inc="ACT")
                    junk_last["s"] = v_sq
                    sl_x1st.note_r("ACT", v_sq)
                    sl_sumsq2.note_w("ACT", v_sq)
                    if jb % 4 == 3:
                        c0b = col - 3
                        pl.wait("vector", "ACT", v_sq)
                        pl.wait("vector", "ACT", v)
                        v_m2 = pl.op("vector", lambda e, c0b=c0b:
                                     e.tensor_scalar_mul(
                                         mean2[:, c0b:c0b + 4],
                                         sums2[:, c0b:c0b + 4], INVD),
                                     inc="DVE")
                        v_v2 = pl.op("vector", lambda e, c0b=c0b:
                                     e.tensor_scalar(
                                         veps2[:, c0b:c0b + 4],
                                         sumsq2[:, c0b:c0b + 4], INVD,
                                         EPS, ALU.mult, ALU.add), inc="DVE")
                        pl.wait("vector", "DVE", v_m2)
                        v_t2 = pl.op("vector", lambda e, c0b=c0b: e.tensor_mul(
                            tmp1[:, 0:4], mean2[:, c0b:c0b + 4],
                            mean2[:, c0b:c0b + 4]), inc="DVE")
                        pl.wait("vector", "DVE", v_t2)
                        v = pl.op("vector", lambda e, c0b=c0b: e.tensor_sub(
                            veps2[:, c0b:c0b + 4], veps2[:, c0b:c0b + 4],
                            tmp1[:, 0:4]), inc="DVE")
                        sl_sumsq2.note_r("DVE", v)
                    if s == 0 and jb == 12:
                        emit_ln_stats(1, 0)
                    elif s == 0 and jb == 13:
                        emit_ln_mid(1, 0)
                    elif s == 0 and jb == 14:
                        emit_ln_norm(1, 0)
                c0 = s * NJB
                pl.wait("scalar", "DVE", pl.cnt["DVE"])
                pl.wait_readers("scalar", sl_std2)
                v_std = pl.op("scalar", lambda e, c0=c0: e.activation(
                    std2[:, c0:c0 + NJB], veps2[:, c0:c0 + NJB], AF.Sqrt),
                    inc="ACT")
                sl_std2.set_w("ACT", v_std)
                pl.wait("vector", "ACT", v_std)
                v_rr = pl.op("vector", lambda e, c0=c0: e.reciprocal(
                    rstd2[:, c0:c0 + NJB], std2[:, c0:c0 + NJB]), inc="DVE")
                pl.wait("vector", "DVE", v_rr)
                v_nr2 = pl.op("vector", lambda e, c0=c0: e.scalar_tensor_tensor(
                    negmr2[:, c0:c0 + NJB], mean2[:, c0:c0 + NJB], -1.0,
                    rstd2[:, c0:c0 + NJB], ALU.mult, ALU.mult), inc="DVE")
                sl_std2.note_r("DVE", v_nr2)
                ln2_ready[s] = v_nr2

            phase_a(0)
            if KDBG:
                dbg_dump("qf", q_f[:, :], [("DVE", pl.cnt["DVE"])], sl_qf)
                dbg_dump("kf", k_f[:, :], [("DVE", pl.cnt["DVE"])], sl_kf)
                dbg_dump("vt", v_t[:, :], [("DVE", pl.cnt["DVE"])], sl_vt)
            phase_b(0)
            if KDBG:
                dbg_dump("mean2", mean2[:, :], [("DVE", pl.cnt["DVE"])])
                dbg_dump("veps2", veps2[:, :], [("DVE", pl.cnt["DVE"])])
                dbg_dump("rstd2", rstd2[:, :], [("DVE", pl.cnt["DVE"])])
                dbg_dump("negmr2", negmr2[:, :], [("DVE", pl.cnt["DVE"])])
            phase_a(1)
            phase_b(1)
            emit_p2(len(p2_jobs))

        cc = ExitStack()

        def sb_c(name, shape, dt):
            return cc.enter_context(nc.sbuf_tensor(name, shape, dt))

        with cc:
            y2 = sb_c("y2", [128, 8 * 1024], BF16)
            hh = sb_c("hh", [128, 24 * 1024], BF16)
            wgt = sb_c("wgt", [128, 2 * 4096], BF16)
            wut = sb_c("wut", [128, 2 * 4096], BF16)
            wdt = sb_c("wdt", [128, 24 * 1024], BF16)
            x1ln = sb_c("x1ln", [128, 6 * 1024], F32)
            x1res = sb_c("x1res", [128, 2 * 1024], F32)
            outst = sb_c("outst", [128, 2 * 1024], F32)
            y2t = sb_c("y2t", [128, 8 * 1024], BF16)
            sg = sb_c("sg", [128, 2 * 512], BF16)

            sl_y2h = [Slot(), Slot()]
            sl_hh = Slot()
            sl_wgt = [Slot() for _ in range(2)]
            sl_wut = [Slot() for _ in range(2)]
            sl_wdt = Slot()
            sl_x1ln = [Slot() for _ in range(6)]
            sl_x1res = [Slot() for _ in range(2)]
            sl_outst = [Slot() for _ in range(2)]
            sl_y2t = [Slot() for _ in range(8)]
            sl_sg = [Slot() for _ in range(2)]

            for sem in (["X1ST", "ID", "WQ", "WK", "WV", "PR0", "PR1", "DBG"]
                        + [f"XP{i}" for i in range(6)]):
                pl.wait("sync", sem, pl.cnt[sem])

            x1ln_ptr = [0]
            x1ln_loaded = {}

            def ensure_x1ln(upto):
                while x1ln_ptr[0] <= min(upto, NTT * 8 - 1):
                    r = x1ln_ptr[0]
                    x1ln_ptr[0] += 1
                    slot = r % 6
                    sem = f"LN{slot}"
                    sl = sl_x1ln[slot]
                    pl.wait_readers("sync", sl)
                    v = pl.dma("sync", sem,
                               x1ln[:, slot * 1024:(slot + 1) * 1024],
                               out_d.ap()[r * 128:r * 128 + 128, :])
                    sl.set_w(sem, v)
                    x1ln_loaded[r] = (slot, (sem, v))

            def load_wchunk(n, c512):
                p = c512 % 2
                for wsrc, buf, sl, sem in ((wgs2, wgt, sl_wgt[p], f"WG{p}"),
                                           (wus2, wut, sl_wut[p], f"WU{p}")):
                    pl.wait_readers("sync", sl)
                    dst = buf[:, p * 4096:(p + 1) * 4096].rearrange(
                        "q (dc h) -> q dc h", dc=8)
                    src = wsrc.ap().rearrange(
                        "(dc q) h -> q dc h",
                        q=128)[:, :, c512 * 512:(c512 + 1) * 512]
                    v = pl.dma("sync", sem, dst, src)
                    sl.set_w(sem, v)

            def ln2_norm(n, rsub):
                r = n * 8 + rsub
                ensure_x1ln(r + 5)
                slot, vld = x1ln_loaded.pop(r)
                xt = x1ln[:, slot * 1024:(slot + 1) * 1024]
                yt = y2t[:, rsub * 1024:(rsub + 1) * 1024]
                pl.wait("vector", *vld)
                pl.wait("vector", "DVE", ln2_ready[r // NJB])
                pl.wait_readers("vector", sl_y2t[rsub])
                v_n = pl.op("vector", lambda e, xt=xt, yt=yt, r=r:
                            e.tensor_scalar(yt, xt, rstd2[:, r:r + 1],
                                            negmr2[:, r:r + 1],
                                            ALU.mult, ALU.add), inc="DVE")
                sl_x1ln[slot].note_r("DVE", v_n)
                sl_y2t[rsub].set_w("DVE", v_n)

            def ln2_trans(n, rsub):
                yt = y2t[:, rsub * 1024:(rsub + 1) * 1024]
                half = rsub % 2
                pt = P_PT[:, half * 1024:(half + 1) * 1024]
                pl.wait_writers("tensor", sl_y2t[rsub])
                pl.wait_readers("tensor", sl_PTh[half])
                v_tr = 0
                for dc in range(DC):
                    inc = "PE" if dc == DC - 1 else None
                    vv = pl.op("tensor", lambda e, yt=yt, dc=dc, pt=pt:
                               e.transpose(
                                   pt[:, dc * 128:(dc + 1) * 128],
                                   yt[:, dc * 128:(dc + 1) * 128],
                                   s_id[:, :]), inc=inc)
                    if vv is not None:
                        v_tr = vv
                sl_PTh[half].set_w("PE", v_tr)
                sl_y2t[rsub].note_r("PE", v_tr)
                th = rsub // 4
                pl.wait("vector", "PE", v_tr)
                pl.wait_readers("vector", sl_y2h[th])
                src = pt.rearrange("p (dc i) -> p dc i", dc=8)
                dst = y2[:, :].rearrange(
                    "p (dc t) -> p dc t", dc=8)[:, :, rsub * 128:(rsub + 1) * 128]
                v_ev = pl.op("vector", lambda e, src=src, dst=dst:
                             e.tensor_copy(dst, src), inc="DVE")
                sl_PTh[half].note_r("DVE", v_ev)
                sl_y2h[th].note_w("DVE", v_ev)

            load_wchunk(0, 0)
            for rsub in range(4):
                ln2_norm(0, rsub)
            for rsub in range(4):
                ln2_trans(0, rsub)
            for rsub in range(4, 8):
                ln2_norm(0, rsub)
            load_wchunk(0, 1)
            pl.wait_readers("sync", sl_wdt)
            v = pl.dma("sync", "WD",
                       wdt[:, :].rearrange("q (hc d) -> q hc d", hc=24),
                       wds2.ap().rearrange("(hc q) d -> q hc d", q=128))
            sl_wdt.set_w("WD", v)
            for rsub in range(4, 8):
                ln2_trans(0, rsub)
            if KDBG:
                dbg_dump("y2", y2[:, :], [("DVE", pl.cnt["DVE"])], sl_y2)

            for n in range(NTT):
                if n + 1 < NTT:
                    for rsub in range(8):
                        ln2_norm(n + 1, rsub)
                for idx in range(HC * 2):
                    hc = idx // 2
                    th = idx % 2
                    c512 = hc // 4
                    hcc = hc % 4
                    p = c512 % 2
                    pl.wait_writers("tensor", sl_wgt[p])
                    pl.wait_writers("tensor", sl_wut[p])
                    pl.wait_writers("tensor", sl_y2h[th])
                    g_slot = (idx * 2) % 4
                    u_slot = (idx * 2 + 1) % 4
                    vgu = {}
                    for wbuf, slot_i, sl_w in ((wgt, g_slot, sl_wgt[p]),
                                               (wut, u_slot, sl_wut[p])):
                        sls = sl_S[slot_i]
                        pl.wait_readers("tensor", sls)
                        v_mm = 0
                        for dc in range(DC):
                            inc = "PE" if dc == DC - 1 else None
                            vv = pl.op(
                                "tensor",
                                lambda e, dc=dc, p=p, hcc=hcc, th=th,
                                slot_i=slot_i, wbuf=wbuf:
                                e.matmul(
                                    P_S[:, slot_i * 512:(slot_i + 1) * 512],
                                    wbuf[:, p * 4096 + dc * 512 + hcc * 128:
                                         p * 4096 + dc * 512 + hcc * 128 + 128],
                                    y2[:, dc * 1024 + th * 512:
                                       dc * 1024 + th * 512 + 512],
                                    start=(dc == 0), stop=(dc == DC - 1)),
                                inc=inc)
                            if vv is not None:
                                v_mm = vv
                        sls.set_w("PE", v_mm)
                        sl_w.note_r("PE", v_mm)
                        sl_y2h[th].note_r("PE", v_mm)
                        vgu[slot_i] = v_mm
                    p2 = idx % 2
                    pl.wait("scalar", "PE", vgu[g_slot])
                    pl.wait_readers("scalar", sl_sg[p2])
                    v_si = pl.op("scalar", lambda e, p2=p2, g_slot=g_slot:
                                 e.activation(
                                     sg[:, p2 * 512:(p2 + 1) * 512],
                                     P_S[:, g_slot * 512:(g_slot + 1) * 512],
                                     AF.Silu, scale=INV127), inc="ACT")
                    sl_S[g_slot].note_r("ACT", v_si)
                    sl_sg[p2].set_w("ACT", v_si)
                    pl.wait("vector", "ACT", v_si)
                    pl.wait("vector", "PE", vgu[u_slot])
                    pl.wait_readers("vector", sl_hh)
                    v_hh = pl.op(
                        "vector",
                        lambda e, hc=hc, th=th, p2=p2, u_slot=u_slot:
                        e.scalar_tensor_tensor(
                            hh[:, hc * 1024 + th * 512:
                               hc * 1024 + th * 512 + 512],
                            P_S[:, u_slot * 512:(u_slot + 1) * 512],
                            INV127, sg[:, p2 * 512:(p2 + 1) * 512],
                            ALU.mult, ALU.mult), inc="DVE")
                    sl_S[u_slot].note_r("DVE", v_hh)
                    sl_sg[p2].note_r("DVE", v_hh)
                    sl_hh.note_w("DVE", v_hh)
                    if th == 1 and hcc == 3:
                        tgt = c512 + 2
                        if tgt < C512:
                            load_wchunk(n, tgt)
                        elif n + 1 < NTT:
                            load_wchunk(n + 1, tgt - C512)
                if n + 1 < NTT:
                    for rsub in range(8):
                        ln2_trans(n + 1, rsub)
                hh_done = max(sl_hh.w.values())
                if KDBG and n == 0:
                    dbg_dump("hh", hh[:, :], [("DVE", hh_done)], sl_hh)
                pl.wait("tensor", "DVE", hh_done)
                pl.wait_writers("tensor", sl_wdt)
                res_loaded = {}

                def load_x1res(tsub):
                    p = tsub % 2
                    sem = f"XR{p}"
                    sl = sl_x1res[p]
                    pl.wait_readers("sync", sl)
                    row0 = n * TT + tsub * 128
                    v = pl.dma("sync", sem,
                               x1res[:, p * 1024:(p + 1) * 1024],
                               out_d.ap()[row0:row0 + 128, :])
                    sl.set_w(sem, v)
                    res_loaded[tsub] = (sem, v)

                load_x1res(0)
                load_x1res(1)
                for tsub in range(8):
                    th = tsub // 4
                    toff = (tsub % 4) * 128
                    po = tsub % 2
                    for dh in range(2):
                        asl = dh
                        sla = sl_A[asl]
                        pl.wait_readers("tensor", sla)
                        v_mm = 0
                        for hc in range(HC):
                            inc = "PE" if hc == HC - 1 else None
                            vv = pl.op(
                                "tensor",
                                lambda e, hc=hc, th=th, toff=toff, dh=dh,
                                asl=asl:
                                e.matmul(
                                    P_ATT[:, asl * 512:(asl + 1) * 512],
                                    hh[:, hc * 1024 + th * 512 + toff:
                                       hc * 1024 + th * 512 + toff + 128],
                                    wdt[:, hc * 1024 + dh * 512:
                                        hc * 1024 + dh * 512 + 512],
                                    start=(hc == 0), stop=(hc == HC - 1)),
                                inc=inc)
                            if vv is not None:
                                v_mm = vv
                        sla.set_w("PE", v_mm)
                        sl_hh.note_r("PE", v_mm)
                        sl_wdt.note_r("PE", v_mm)
                        pl.wait("vector", "PE", v_mm)
                        pl.wait("vector", *res_loaded[tsub])
                        pl.wait_readers("vector", sl_outst[po])
                        v_ev = pl.op(
                            "vector",
                            lambda e, asl=asl, po=po, dh=dh:
                            e.scalar_tensor_tensor(
                                outst[:, po * 1024 + dh * 512:
                                      po * 1024 + dh * 512 + 512],
                                P_ATT[:, asl * 512:(asl + 1) * 512],
                                INV127,
                                x1res[:, po * 1024 + dh * 512:
                                      po * 1024 + dh * 512 + 512],
                                ALU.mult, ALU.add), inc="DVE")
                        sl_A[asl].note_r("DVE", v_ev)
                        sl_x1res[po].note_r("DVE", v_ev)
                        sl_outst[po].note_w("DVE", v_ev)
                    row0 = n * TT + tsub * 128
                    pl.wait("sync", "DVE", max(sl_outst[po].w.values()))
                    v_st = pl.dma("sync", f"OS{po}",
                                  out_d.ap()[row0:row0 + 128, :],
                                  outst[:, po * 1024:(po + 1) * 1024])
                    sl_outst[po].note_r(f"OS{po}", v_st)
                    if tsub + 2 < 8:
                        load_x1res(tsub + 2)

            for sem in ("X1ST", "OS0", "OS1", "DBG"):
                pl.wait("sync", sem, pl.cnt[sem])
            pl.flush_block(nc)

    return nc


def _get_nc():
    return build_program()


def kernel(L0_attn_x, L0_attn_W_query, L0_attn_W_key, L0_attn_W_value,
           L0_ffn_w_gate, L0_ffn_w_up, L0_ffn_w_down):
    x = np.ascontiguousarray(np.asarray(L0_attn_x, dtype=np.float32))
    wq = np.ascontiguousarray(np.asarray(L0_attn_W_query, dtype=np.float32))
    wk = np.ascontiguousarray(np.asarray(L0_attn_W_key, dtype=np.float32))
    wv = np.ascontiguousarray(np.asarray(L0_attn_W_value, dtype=np.float32))
    wg = np.ascontiguousarray(np.asarray(L0_ffn_w_gate, dtype=np.int32))
    wu = np.ascontiguousarray(np.asarray(L0_ffn_w_up, dtype=np.int32))
    wd = np.ascontiguousarray(np.asarray(L0_ffn_w_down, dtype=np.int32))

    nc = _get_nc()
    in_maps = []
    for c in range(NCORES):
        shard = np.ascontiguousarray(
            x[NSEQ * c:NSEQ * (c + 1)].reshape(T, D))
        in_maps.append({"x": shard, "wq": wq, "wk": wk, "wv": wv,
                        "wg": wg, "wu": wu, "wd": wd})
    res = run_bass_kernel_spmd(nc, in_maps, core_ids=list(range(NCORES)))
    outs = [np.asarray(r["out"], dtype=np.float32).reshape(NSEQ, SEQ, D)
            for r in res.results]
    return np.concatenate(outs, axis=0)
